# version 05
# baseline (speedup 1.0000x reference)
import numpy as np
import concourse.bass as bass
import concourse.mybir as mybir
from concourse.bass_utils import run_bass_kernel_spmd

F32 = mybir.dt.float32
BF16 = mybir.dt.bfloat16
AF = mybir.ActivationFunctionType
ALU = mybir.AluOpType
AX = mybir.AxisListType

B, T, D = 2, 4096, 4096
EPS_LN = 1e-5
EPS_RMS = 1.1920929e-07
SCALE0 = 4.0 / float(np.sqrt(1024.0))
SC1 = 1.0 / float(np.sqrt(128.0))
SINK_IT = 50
TAU_INV = 10.0
NEG_BIG = -1e30
SCALING = 2.0

_cache = {}


STAGE_HOLDER = [99]


def build_program(stage=99):
    STAGE_HOLDER[0] = stage
    nc = bass.Bass("TRN2", target_bir_lowering=False)

    xt = nc.dram_tensor("xt", [T, D], F32, kind="ExternalInput").ap()
    xtr = nc.dram_tensor("xtr", [D, T], F32, kind="ExternalInput").ap()
    xo = nc.dram_tensor("xo", [1024, D], F32, kind="ExternalInput").ap()
    xotr = nc.dram_tensor("xotr", [D, 1024], F32, kind="ExternalInput").ap()
    bvq = nc.dram_tensor("bvq", [4, 1024], F32, kind="ExternalInput").ap()
    bvk = nc.dram_tensor("bvk", [4, 1024], F32, kind="ExternalInput").ap()
    bvv = nc.dram_tensor("bvv", [4, 1024], F32, kind="ExternalInput").ap()
    wst = nc.dram_tensor("wst", [128, 32 * 24], F32, kind="ExternalInput").ap()
    bst = nc.dram_tensor("bst", [24], F32, kind="ExternalInput").ap()
    gate = nc.dram_tensor("gate", [1], F32, kind="ExternalInput").ap()
    wbr = nc.dram_tensor("wbr", [8, 128, 1024], F32, kind="ExternalInput").ap()
    wqh = nc.dram_tensor("wqh", [32, 128, 4096], F32, kind="ExternalInput").ap()
    bq = nc.dram_tensor("bq", [128, 32], F32, kind="ExternalInput").ap()
    wkp = nc.dram_tensor("wkp", [128, 32 * 128], F32, kind="ExternalInput").ap()
    bkv = nc.dram_tensor("bkv", [2, 128], F32, kind="ExternalInput").ap()
    wvp = nc.dram_tensor("wvp", [128, 32 * 128], F32, kind="ExternalInput").ap()
    at_ = nc.dram_tensor("at_", [32, 128, 1024], F32, kind="ExternalInput").ap()
    bt_ = nc.dram_tensor("bt_", [1024, 4096], F32, kind="ExternalInput").ap()
    wrp = nc.dram_tensor("wrp", [128, 32 * 64], F32, kind="ExternalInput").ap()
    out = nc.dram_tensor("out", [1024, D], F32, kind="ExternalOutput").ap()
    x1s = nc.dram_tensor("x1s", [1024, D], F32).ap()
    xn1ts = nc.dram_tensor("xn1ts", [128, 32 * 1024], BF16).ap()
    wsc = nc.dram_tensor("wsc", [2, D], F32).ap()

    import contextlib
    ctx = contextlib.ExitStack()
    with ctx:
        def sb(name, shape, dt):
            return ctx.enter_context(nc.sbuf_tensor(name, shape, dt))

        ident = sb("ident", [128, 128], BF16)
        identf = sb("identf", [128, 128], F32)
        onescol = sb("onescol", [128, 1], BF16)
        onesrow = sb("onesrow", [1, 128], F32)
        onesrowb = sb("onesrowb", [1, 128], BF16)
        one1b = sb("one1b", [1, 1], BF16)
        stat_m = sb("stat_m", [128, 48], F32)
        stat_nm = sb("stat_nm", [128, 48], F32)
        stat_r = sb("stat_r", [128, 48], F32)
        stat_mb = sb("stat_mb", [128, 48], BF16)
        own_nm = sb("own_nm", [128, 8], F32)
        own_r = sb("own_r", [128, 8], F32)
        own_rrms = sb("own_rrms", [128, 8], F32)
        own_nmrr = sb("own_nmrr", [128, 8], F32)
        o1_nm = sb("o1_nm", [128, 8], F32)
        o1_r = sb("o1_r", [128, 8], F32)
        csb = sb("csb", [1, 2048], F32)
        crowb = sb("crowb", [1, 2048], BF16)
        rrow = sb("rrow", [1, 512], F32)
        growq = sb("growq", [1, 512], F32)
        ncolT = sb("ncolT", [128, 32], F32)
        g0col = sb("g0col", [128, 1], F32)
        g0ncol = sb("g0ncol", [128, 1], F32)
        sm = sb("sm", [128, 64], F32)
        w_dcol = sb("w_dcol", [128, 32], F32)
        wcolrep = sb("wcolrep", [128, 24], F32)
        biasrep = sb("biasrep", [128, 24], F32)
        lsb = sb("lsb", [128, 96], F32)
        skZ = sb("skZ", [128, 64], F32)
        skE = sb("skE", [128, 64], F32)
        skM = sb("skM", [128, 64], F32)
        skU = sb("skU", [128, 16], F32)
        skV = sb("skV", [128, 16], F32)
        hpre = sb("hpre", [128, 16], F32)
        hpost = sb("hpost", [128, 16], F32)
        rep16 = sb("rep16", [16, 512], BF16)
        repG = sb("repG", [64, 1024], BF16)
        gtsb = sb("gtsb", [64, 1024], BF16)
        otsb = sb("otsb", [16, 8 * 1024], BF16)
        bkrow = sb("bkrow", [1, 128], BF16)
        bvrow = sb("bvrow", [1, 128], BF16)
        bqrow = sb("bqrow", [1, 128], BF16)
        wsb32 = sb("wsb32", [32, 128], F32)
        r24 = sb("r24", [1, 24], F32)
        bqcol_b = sb("bqcol_b", [128, 1], BF16)
        gb_tile = sb("gb_tile", [128, 64], BF16)

        AR_W = 41984
        arena = sb("arena", [128, AR_W], F32)
        arb = arena[:, :].bitcast(BF16)

        def af(off, n):
            return arena[:, off:off + n]

        def ab(off, n):
            return arb[:, 2 * off:2 * off + n]

        O_ACC, O_QWT, O_STV, O_STK = 0, 8192, 16384, 20480
        O_XVB, O_XKB, O_ATT, O_GRP = 24576, 32768, 34816, 38912
        O_T1, O_JNK, O_SQ = 39936, 40960, 41920
        O_QTK = O_XVB
        C_XTR, C_VBR, C_XPR = O_QWT, O_STV, O_XVB
        C_XPT, C_YT, C_Y = O_XVB + 4096, O_XVB + 6144, O_XKB
        C_ATL, C_XF = O_ATT, O_ATT + 2048
        C_WBB, C_WST, C_XN1 = O_STK, O_STK + 1024, O_STK + 2048
        L_XNT, L_ATB, L_HT = 0, 16384, 17408
        L_CC, L_K, L_VF, L_VB = 21504, 22528, 23552, 24576
        L_OO, L_NUM, L_DEN = 25088, 26112, 27136
        L_ARG, L_EB, L_G = 28160, 30208, 31232
        L_WRB, L_BTB, L_SCO = 33792, 34816, 35328
        L_WQB, L_WKB, L_WVB = 35840, 37888, 39936
        L_WQ = L_ARG
        L_X1T, L_OSB = L_XNT, L_XNT + 1024

        ps = [ctx.enter_context(nc.psum_tensor(f"ps{i}", [128, 512], F32))
              for i in range(8)]

        sems = {}
        for s in ["dv", "dk", "sq", "va", "sd", "vb", "kb", "sc", "fx", "sg",
                  "ts", "at", "av", "ev", "cm", "cv", "qv", "qt", "qe", "qg",
                  "g1", "g2", "g3", "g4", "g5", "g6", "g7", "g8", "ou", "gq"]:
            sems[s] = ctx.enter_context(nc.semaphore("s_" + s))
        cnt = {k: 0 for k in sems}
        gv = [0]

        def W(eng, s, val):
            eng.wait_ge(sems[s], val)

        def I(instr, s, n=1):
            instr.then_inc(sems[s], n)
            cnt[s] += n
            return instr

        def finish_early():
            with nc.Block() as blk:
                b_cm = cnt["cm"]

                @blk.gpsimd
                def _(g):
                    g.memset(af(0, 4096), 0.0)
                    I(g.drain(), "cm", 1)

                @blk.sync
                def _(sy):
                    W(sy, "cm", b_cm + 1)
                    for r in range(8):
                        I(sy.dma_start(out[r * 128:(r + 1) * 128, :],
                                       af(0, 4096)), "ou", 16)
                    sy.wait_ge(sems["ou"], cnt["ou"])
            return nc

        _DPOPS = {"tensor_scalar", "tensor_tensor", "tensor_add", "tensor_sub",
                  "tensor_mul", "tensor_max", "tensor_copy", "reduce_sum",
                  "reduce_max", "tensor_reduce", "reciprocal",
                  "scalar_tensor_tensor", "max", "memset", "activation"}

        class DP:
            def __init__(self, e):
                self.e = e

            def __getattr__(self, n):
                f = getattr(self.e, n)
                if n in _DPOPS:
                    def g(*a, **k):
                        r = f(*a, **k)
                        self.e.drain()
                        return r
                    return g
                return f

        def newton(v, rcol, acol, t0):
            v = DP(v)
            v.tensor_tensor(t0, rcol, rcol, ALU.mult)
            v.tensor_tensor(t0, t0, acol, ALU.mult)
            v.tensor_scalar(t0, t0, -0.5, 1.5, ALU.mult, ALU.add)
            v.tensor_tensor(rcol, rcol, t0, ALU.mult)

        def emit_stats_scalar(s, sl):
            for h4 in range(4):
                if h4:
                    s.drain()
                s.activation(ab(O_JNK, 1024), af(O_STV + h4 * 1024, 1024),
                             AF.Square, accum_out=af(O_SQ + h4, 1))
            I(s.drain(), "sq", 1)

        def emit_sqrt_scalar(s, sl):
            s.activation(af(O_SQ + 6 + sl, 1), af(O_SQ + 4 + sl, 1), AF.Sqrt)
            I(s.drain(), "sd", 1)

        def emit_stats_vector(v, sl, nmcol, vsave=None):
            v = DP(v)
            v.reduce_sum(sm[:, 2:3], af(O_STV, 4096), axis=AX.X)
            v.tensor_add(af(O_SQ, 1), af(O_SQ, 1), af(O_SQ + 1, 1))
            v.tensor_add(af(O_SQ + 2, 1), af(O_SQ + 2, 1), af(O_SQ + 3, 1))
            v.tensor_add(af(O_SQ, 1), af(O_SQ, 1), af(O_SQ + 2, 1))
            v.tensor_scalar(sm[:, 2:3], sm[:, 2:3], 1.0 / D, None, ALU.mult)
            v.tensor_scalar(af(O_SQ, 1), af(O_SQ, 1), 1.0 / D, None, ALU.mult)
            v.tensor_tensor(sm[:, 3:4], sm[:, 2:3], sm[:, 2:3], ALU.mult)
            v.tensor_sub(af(O_SQ + 4 + sl, 1), af(O_SQ, 1), sm[:, 3:4])
            v.tensor_scalar(af(O_SQ + 4 + sl, 1), af(O_SQ + 4 + sl, 1), 1.0,
                            EPS_LN, ALU.mult, ALU.add)
            I(v.drain(), "va", 1)
            v.tensor_scalar(nmcol, sm[:, 2:3], -1.0, None, ALU.mult)
            if vsave is not None:
                v.tensor_copy(vsave, af(O_SQ + 4 + sl, 1))

        def emit_r_vector(v, sl, rcol):
            v = DP(v)
            v.reciprocal(rcol, af(O_SQ + 6 + sl, 1))
            if STAGE_HOLDER[0] != -24:
                newton(v, rcol, af(O_SQ + 4 + sl, 1), sm[:, 4:5])

        with nc.Block() as blk:
            @blk.sync
            def _(sy):
                I(sy.dma_start(af(O_STV, 1024)[0:4, :], bvq), "g1", 16)
                I(sy.dma_start(af(O_STV + 1024, 1024)[0:4, :], bvk), "g1", 16)
                I(sy.dma_start(af(O_STV + 2048, 1024)[0:4, :], bvv), "g1", 16)
                I(sy.dma_start(sm[0:1, 0:1], gate[0:1]), "g1", 16)

            @blk.gpsimd
            def _(g):
                for t_ in (ident, identf):
                    g.memset(t_[:, :], 1.0)
                    g.drain()
                    g.affine_select(t_[:, :], t_[:, :], [[1, 128]], ALU.is_ge,
                                    0.0, base=0, channel_multiplier=-1)
                    g.drain()
                    g.affine_select(t_[:, :], t_[:, :], [[-1, 128]], ALU.is_ge,
                                    0.0, base=0, channel_multiplier=1)
                    g.drain()
                g.memset(onescol[:, :], 1.0)
                g.memset(onesrow[:, :], 1.0)
                g.memset(onesrowb[:, :], 1.0)
                g.memset(one1b[:, :], 1.0)
                g.memset(rep16[:, :], 1.0)
                g.drain()
                r16 = rep16[:, :].rearrange("p (a b) -> p a b", a=16)
                g.affine_select(r16, r16, [[1, 16], [0, 32]], ALU.is_ge, 0.0,
                                base=0, channel_multiplier=-1)
                g.drain()
                g.affine_select(r16, r16, [[-1, 16], [0, 32]], ALU.is_ge, 0.0,
                                base=0, channel_multiplier=1)
                g.drain()
                g.memset(repG[:, :], 1.0)
                g.drain()
                rg = repG[:, :].rearrange("p (s m e) -> p s m e", s=8, m=8)
                g.affine_select(rg, rg, [[8, 8], [1, 8], [0, 16]], ALU.is_ge,
                                0.0, base=0, channel_multiplier=-1)
                g.drain()
                g.affine_select(rg, rg, [[-8, 8], [-1, 8], [0, 16]], ALU.is_ge,
                                0.0, base=0, channel_multiplier=1)
                g.drain()
                g.memset(csb[:, :], 0.0)
                g.memset(sm[:, 40:41], EPS_LN)

        if stage <= -6:
            return finish_early()
        bindv = af(O_STV, 3072)

        with nc.Block() as blk:
            @blk.vector
            def _(v):
                v = DP(v)
                W(v, "g1", 64)
                for i in range(3):
                    v.reduce_sum(sm[0:4, 8 + i:9 + i],
                                 bindv[0:4, i * 1024:(i + 1) * 1024],
                                 axis=AX.X, apply_absolute_value=True)
                    v.tensor_scalar(sm[0:4, 8 + i:9 + i], sm[0:4, 8 + i:9 + i],
                                    1.0 / 1024.0, None, ALU.mult)
                I(v.drain(), "g2", 1)

            @blk.scalar
            def _(s):
                W(s, "g2", 1)
                for i in range(3):
                    s.activation(bindv[0:4, i * 1024:(i + 1) * 1024],
                                 bindv[0:4, i * 1024:(i + 1) * 1024], AF.Sign)
                s.activation(g0col[0:1, 0:1], sm[0:1, 0:1], AF.Sigmoid)
                I(s.drain(), "g2", 1)

        if stage <= -5:
            return finish_early()
        with nc.Block() as blk:
            @blk.vector
            def _(v):
                v = DP(v)
                W(v, "g2", 2)
                for i in range(3):
                    sl = bindv[0:4, i * 1024:(i + 1) * 1024]
                    tmp = af(O_XVB, 1024)[0:4, :]
                    v.tensor_tensor(tmp, sl, sl, ALU.mult)
                    v.tensor_scalar(tmp, tmp, -1.0, 1.0, ALU.mult, ALU.add)
                    v.tensor_add(sl, sl, tmp)
                    v.tensor_scalar(sl, sl, sm[0:4, 8 + i:9 + i], None, ALU.mult)
                v.tensor_tensor(bindv[0:4, 0:1024], bindv[0:4, 0:1024],
                                bindv[0:4, 1024:2048], ALU.mult)
                v.tensor_scalar(bindv[0:4, 0:1024], bindv[0:4, 0:1024], SCALE0,
                                None, ALU.mult)


        if stage <= -4:
            return finish_early()
        b_g1w = cnt["g1"]
        b_g2w = cnt["g2"]
        with nc.Block() as blk:
            @blk.sync
            def _(sy):
                I(sy.dma_start(wsc[0], bindv[0:4, 0:1024]), "g1", 16)
                I(sy.dma_start(wsc[1], bindv[0:4, 2048:3072]), "g1", 16)
                sy.wait_ge(sems["g1"], b_g1w + 32)
                I(sy.dma_start(wsb32[:, :],
                               wsc[0:1, :].rearrange("o (c p) -> (o c) p", c=32)),
                  "g1", 16)

            @blk.tensor
            def _(t):
                W(t, "g1", b_g1w + 48)
                t.transpose(ps[0][:, 0:32], wsb32[:, :], identf[0:32, 0:32])
                t.matmul(ps[1][:, 0:1], onesrow[:, :], g0col[0:1, 0:1],
                         start=True, stop=True)
                I(t.drain(), "g2", 1)

            @blk.vector
            def _(v):
                v = DP(v)
                W(v, "g2", b_g2w + 1)
                v.tensor_copy(w_dcol[:, :], ps[0][:, 0:32])
                v.tensor_copy(sm[:, 1:2], ps[1][:, 0:1])
                v.tensor_scalar(g0ncol[:, 0:1], sm[:, 1:2], -1.0, 1.0,
                                ALU.mult, ALU.add)
                v.tensor_copy(g0col[:, 0:1], sm[:, 1:2])
                I(v.drain(), "g2", 1)

        if stage <= -3:
            return finish_early()
        ST_NV = (stage == -21)
        ST_NS = (stage == -22)
        if stage in (-23, -24, -25):
            pass
        with nc.Block() as blk:
            b_dv, b_sq, b_va, b_sd = cnt["dv"], cnt["sq"], cnt["va"], cnt["sd"]

            @blk.sync
            def _(sy):
                for i in range(8):
                    if gv[0] > 0 and not ST_NV:
                        W(sy, "vb", 2 * gv[0])
                    elif gv[0] > 0:
                        W(sy, "vb", gv[0])
                    I(sy.dma_start(af(O_STV, 4096), xo[i * 128:(i + 1) * 128, :]),
                      "dv", 16)
                    gv[0] += 1

            @blk.scalar
            def _(s):
                for i in range(8):
                    W(s, "dv", b_dv + 16 * (i + 1))
                    if i > 0 and not ST_NV:
                        W(s, "va", b_va + i)
                    emit_stats_scalar(s, 0)
                    I(s.drain(), "vb", 1)
                    if not ST_NV and not ST_NS:
                        W(s, "va", b_va + i + 1)
                        emit_sqrt_scalar(s, 0)

            if not ST_NV:
                @blk.vector
                def _(v):
                    for i in range(8):
                        W(v, "dv", b_dv + 16 * (i + 1))
                        W(v, "sq", b_sq + i + 1)
                        emit_stats_vector(v, 0, own_nm[:, i:i + 1],
                                          sm[:, 8 + i:9 + i])
                        I(v.drain(), "vb", 1)
                        if not ST_NS:
                            W(v, "sd", b_sd + i + 1)
                            emit_r_vector(v, 0, own_r[:, i:i + 1])
        if stage in (-21, -22, -23, -24, -25):
            return finish_early()

        with nc.Block() as blk:
            b_g3 = cnt["g3"]

            @blk.vector
            def _(v):
                v = DP(v)
                v.tensor_tensor(sm[:, 24:32], sm[:, 8:16], own_r[:, :], ALU.mult)
                v.tensor_tensor(sm[:, 24:32], sm[:, 24:32], own_r[:, :], ALU.mult)
                v.tensor_scalar(sm[:, 16:24], sm[:, 8:16], 1.0, -EPS_LN,
                                ALU.mult, ALU.add)
                v.tensor_tensor(sm[:, 24:32], sm[:, 16:24], own_r[:, :], ALU.mult)
                v.tensor_tensor(sm[:, 24:32], sm[:, 24:32], own_r[:, :], ALU.mult)
                v.tensor_scalar(sm[:, 24:32], sm[:, 24:32], 1.0, EPS_RMS,
                                ALU.mult, ALU.add)
                I(v.drain(), "g3", 1)

            @blk.scalar
            def _(s):
                W(s, "g3", b_g3 + 1)
                s.activation(sm[:, 32:40], sm[:, 24:32], AF.Sqrt)
                I(s.drain(), "g3", 1)

            @blk.vector
            def _(v):
                v = DP(v)
                W(v, "g3", b_g3 + 2)
                v.reciprocal(own_rrms[:, :], sm[:, 32:40])
                for i in range(8):
                    newton(v, own_rrms[:, i:i + 1], sm[:, 24 + i:25 + i], sm[:, 4:5])
                v.tensor_tensor(own_rrms[:, :], own_rrms[:, :], own_r[:, :],
                                ALU.mult)
                v.tensor_tensor(own_nmrr[:, :], own_nm[:, :], own_rrms[:, :],
                                ALU.mult)

        def build_qw(chunk):
            with nc.Block() as blk:
                b_dv = cnt["dv"]
                b_qv, b_qt, b_qe = cnt["qv"], cnt["qt"], cnt["qe"]
                b_av = cnt["av"]

                @blk.sync
                def _(sy):
                    for tl in range(4):
                        W(sy, "vb", 2 * gv[0])
                        I(sy.dma_start(
                            af(O_STV, 4096),
                            xo[(chunk * 4 + tl) * 128:(chunk * 4 + tl + 1) * 128, :]),
                            "dv", 16)
                        gv[0] += 1

                @blk.vector
                def _(v):
                    for tl in range(4):
                        i = chunk * 4 + tl
                        W(v, "dv", b_dv + 16 * (tl + 1))
                        if tl == 0:
                            W(v, "av", b_av)
                        v.tensor_scalar(ab(O_QTK, 2048), af(O_STV, 2048),
                                        own_nm[:, i:i + 1], own_r[:, i:i + 1],
                                        ALU.add, ALU.mult)
                        v.tensor_scalar(ab(O_QTK + 1024, 2048),
                                        af(O_STV + 2048, 2048),
                                        own_nm[:, i:i + 1], own_r[:, i:i + 1],
                                        ALU.add, ALU.mult)
                        I(v.drain(), "vb", 2)
                        I(v.drain(), "qv", 1)
                        for c4 in range(8):
                            W(v, "qt", b_qt + tl * 8 + c4 + 1)
                            pv = ps[6 + (c4 % 2)][:, :].bitcast(BF16)
                            for cc in range(4):
                                c = c4 * 4 + cc
                                v.tensor_scalar(
                                    ab(O_QWT + (c * 512 + tl * 128) // 2, 128),
                                    pv[:, cc * 128:(cc + 1) * 128],
                                    w_dcol[:, c:c + 1], None, ALU.mult)
                            I(v.drain(), "qe", 1)

                @blk.tensor
                def _(t):
                    for tl in range(4):
                        W(t, "qv", b_qv + tl + 1)
                        for c4 in range(8):
                            if tl * 8 + c4 >= 2:
                                W(t, "qe", b_qe + tl * 8 + c4 - 1)
                            pv = ps[6 + (c4 % 2)][:, :].bitcast(BF16)
                            for cc in range(4):
                                c = c4 * 4 + cc
                                t.transpose(pv[:, cc * 128:(cc + 1) * 128],
                                            ab(O_QTK + c * 64, 128), ident[:, :])
                            I(t.drain(), "qt", 1)

            with nc.Block() as blk:
                b_g4, b_g5 = cnt["g4"], cnt["g5"]

                @blk.tensor
                def _(t):
                    for h in range(4):
                        for i in range(8):
                            t.matmul(ps[0][0:1, :], onescol[:, :],
                                     ab(O_QWT + (h * 8 + i) * 256, 512),
                                     start=(i == 0), stop=(i == 7))
                        I(t.drain(), "g4", 1)
                        W(t, "g5", b_g5 + 2 * h + 1)
                        t.matmul(ps[1][:, :], onesrow[:, :], growq[0:1, :],
                                 start=True, stop=True)
                        I(t.drain(), "g4", 1)

                @blk.vector
                def _(v):
                    for h in range(4):
                        W(v, "g4", b_g4 + 2 * h + 1)
                        v.tensor_copy(growq[0:1, :], ps[0][0:1, :])
                        I(v.drain(), "g5", 1)
                        W(v, "g4", b_g4 + 2 * h + 2)
                        v.tensor_copy(ab(O_GRP + h * 256, 512), ps[1][:, :])
                        I(v.drain(), "g5", 1)
                    I(v.drain(), "qg", 1)

        def attn_chunk(cpos, ngrp, bs, ngrp_max):
            tn, vc, sc_, sy, gp = (nc.tensor, nc.vector, nc.scalar, nc.sync,
                                   nc.gpsimd)
            q0 = cpos * 512
            nun = ngrp * 4
            nun_max = ngrp_max * 4

            lgv = bs["gv"]
            for ul in range(nun):
                W(sy, "vb", 2 * lgv)
                I(sy.dma_start(af(O_STV, 4096), xt[ul * 128:ul * 128 + 128, :]),
                  "dv", 16)
                lgv += 1
                for half in range(2):
                    mk = 2 * ul + half
                    if mk >= 2:
                        W(sy, "kb", bs["kb"] + mk - 1)
                    I(sy.dma_start(
                        af(O_STK + half * 2048, 2048)
                        .rearrange("p (a t) -> p a t", a=16),
                        xtr[half * 2048:(half + 1) * 2048, ul * 128:ul * 128 + 128]
                        .rearrange("(a p) t -> p a t", p=128)), "dk", 16)
            if nun < nun_max:
                d = nun_max - nun
                I(sy.drain(), "dv", 16 * d)
                I(sy.drain(), "dk", 32 * d)
            gv[0] = lgv + (nun_max - nun)

            for ul in range(nun):
                U = bs["U"] + ul
                W(sc_, "dv", 16 * (bs["gv"] + ul + 1))
                if ul > 0:
                    W(sc_, "va", bs["va"] + ul)
                emit_stats_scalar(sc_, ul % 2)
                I(sc_.drain(), "vb", 1)
                W(sc_, "va", bs["va"] + ul + 1)
                emit_sqrt_scalar(sc_, ul % 2)
                for h in range(4):
                    m = 4 * ul + h
                    if (ul % 4) == 0 and h == 0 and (bs["G"] + ul // 4) > 0:
                        W(sc_, "av", 16 * (bs["G"] + ul // 4))
                    W(sc_, "fx", bs["fx"] + m + 1)
                    sc_.activation(ab(O_ATT + ((ul % 4) * 4 + h) * 256, 512),
                                   af(O_T1 + (m % 2) * 512, 512),
                                   AF.Sigmoid, scale=stat_r[:, U:U + 1])
                    I(sc_.drain(), "sg", 1)
            if nun < nun_max:
                d = nun_max - nun
                I(sc_.drain(), "sq", d)
                I(sc_.drain(), "sd", d)
                I(sc_.drain(), "vb", d)
                I(sc_.drain(), "sg", 4 * d)

            for ul in range(nun):
                U = bs["U"] + ul
                G = bs["G"] + ul // 4
                W(vc, "dv", 16 * (bs["gv"] + ul + 1))
                W(vc, "sq", bs["sq"] + ul + 1)
                emit_stats_vector(vc, ul % 2, stat_nm[:, U:U + 1])
                vc.tensor_scalar(stat_m[:, U:U + 1], stat_nm[:, U:U + 1], -1.0,
                                 None, ALU.mult)
                vc.drain()
                vc.tensor_copy(stat_mb[:, U:U + 1], stat_m[:, U:U + 1])
                vc.drain()
                W(vc, "sd", bs["sd"] + ul + 1)
                emit_r_vector(vc, ul % 2, stat_r[:, U:U + 1])
                if (ul % 4) == 0 and G > 0:
                    W(vc, "av", 16 * G)
                vc.tensor_copy(ab(O_XVB + (ul % 4) * 2048, 4096), af(O_STV, 4096))
                I(vc.drain(), "vb", 1)
                for half in range(2):
                    mk = 2 * ul + half
                    W(vc, "dk", bs["dk"] + 16 * (2 * ul + 2))
                    if mk >= 2:
                        W(vc, "sc", bs["sc"] + 4 * (ul - 1) + 2 * (half + 1))
                    vc.tensor_copy(ab(O_XKB + (mk % 2) * 1024, 2048),
                                   af(O_STK + half * 2048, 2048))
                    I(vc.drain(), "kb", 1)
                for h in range(4):
                    m = 4 * ul + h
                    W(vc, "sc", bs["sc"] + m + 1)
                    if m >= 2:
                        W(vc, "sg", bs["sg"] + m - 1)
                    vc.scalar_tensor_tensor(af(O_T1 + (m % 2) * 512, 512),
                                            ab(O_GRP + h * 256, 512),
                                            stat_nm[:, U:U + 1],
                                            ps[m % 2][:, :], ALU.mult, ALU.add)
                    I(vc.drain(), "fx", 1)
                for h in range(4):
                    m = 4 * ul + h
                    W(vc, "sg", bs["sg"] + m + 1)
                    sl = ab(O_ATT + ((ul % 4) * 4 + h) * 256, 512)
                    vc.tensor_scalar(sl, sl, stat_r[:, U:U + 1], None, ALU.mult)
                    I(vc.drain(), "ts", 1)
                if (ul % 4) == 3:
                    gl = ul // 4
                    for h in range(4):
                        W(vc, "cm", bs["cm"] + 4 * gl + h + 1)
                        dst = csb[0:1, h * 512:(h + 1) * 512]
                        if gl == 0:
                            vc.tensor_copy(dst, ps[2 + (h % 2)][0:1, :])
                        else:
                            vc.tensor_add(dst, dst, ps[2 + (h % 2)][0:1, :])
                        I(vc.drain(), "cv", 1)
                    for st_l in range(16):
                        st = 16 * gl + st_l
                        h, rem = st_l // 4, st_l % 4
                        half, qh = rem // 2, rem % 2
                        W(vc, "av", 16 * bs["G"] + st + 1)
                        for qs in range(2):
                            qsub = qh * 2 + qs
                            dst = ab(O_ACC + (qsub * 4096 + h * 1024 + half * 512)
                                     // 2, 512)
                            src = ps[4 + 2 * (st % 2) + qs][:, :]
                            if gl == 0:
                                vc.tensor_copy(dst, src)
                            else:
                                vc.tensor_add(dst, dst, src)
                        I(vc.drain(), "ev", 2)
            if nun < nun_max:
                d = nun_max - nun
                dg = ngrp_max - ngrp
                I(vc.drain(), "va", d)
                I(vc.drain(), "vb", d)
                I(vc.drain(), "kb", 2 * d)
                I(vc.drain(), "fx", 4 * d)
                I(vc.drain(), "ts", 4 * d)
                I(vc.drain(), "cv", 4 * dg)
                I(vc.drain(), "ev", 32 * dg)

            for ul in range(nun):
                diag = (ul // 4 == ngrp - 1)
                for h in range(4):
                    m = 4 * ul + h
                    W(gp, "ts", bs["ts"] + m + 1)
                    if diag:
                        sl = ab(O_ATT + ((ul % 4) * 4 + h) * 256, 512)
                        gp.affine_select(sl, sl, [[1, 512]], ALU.is_ge, 0.0,
                                         base=q0 - ul * 128, channel_multiplier=-1)
                    I(gp.drain(), "at", 1)
            if nun < nun_max:
                I(gp.drain(), "at", 4 * (nun_max - nun))

            W(tn, "qg", bs["qg"])
            for gl in range(ngrp):
                for ul4 in range(4):
                    ul = gl * 4 + ul4
                    for h in range(4):
                        m = 4 * ul + h
                        W(tn, "kb", bs["kb"] + 2 * ul + (h // 2) + 1)
                        if m >= 2:
                            W(tn, "fx", bs["fx"] + m - 1)
                        for i in range(8):
                            dc = h * 8 + i
                            tn.matmul(ps[m % 2][:, :],
                                      ab(O_XKB + ((2 * ul + dc // 16) % 2) * 1024
                                         + (dc % 16) * 64, 128),
                                      ab(O_QWT + dc * 256, 512),
                                      start=(i == 0), stop=(i == 7))
                        I(tn.drain(), "sc", 1)
                W(tn, "at", bs["at"] + 16 * (gl + 1))
                for h in range(4):
                    idx = 4 * gl + h
                    if idx >= 2:
                        W(tn, "cv", bs["cv"] + idx - 1)
                    for ul4 in range(4):
                        U = bs["U"] + gl * 4 + ul4
                        tn.matmul(ps[2 + (h % 2)][0:1, :], stat_mb[:, U:U + 1],
                                  ab(O_ATT + (ul4 * 4 + h) * 256, 512),
                                  start=(ul4 == 0), stop=(ul4 == 3))
                    I(tn.drain(), "cm", 1)
                for st_l in range(16):
                    st = 16 * gl + st_l
                    h, rem = st_l // 4, st_l % 4
                    half, qh = rem // 2, rem % 2
                    if st >= 2:
                        W(tn, "ev", bs["ev"] + 2 * (st - 1))
                    for ul4 in range(4):
                        for qs in range(2):
                            qsub = qh * 2 + qs
                            tn.matmul(ps[4 + 2 * (st % 2) + qs][:, :],
                                      ab(O_ATT + (ul4 * 4 + h) * 256 + qsub * 64,
                                         128),
                                      ab(O_XVB + ul4 * 2048
                                         + (h * 1024 + half * 512) // 2, 512),
                                      start=(ul4 == 0), stop=(ul4 == 3))
                    I(tn.drain(), "av", 1)
            if ngrp < ngrp_max:
                d = nun_max - nun
                dg = ngrp_max - ngrp
                I(tn.drain(), "sc", 4 * d)
                I(tn.drain(), "cm", 4 * dg)
                I(tn.drain(), "av", 16 * dg)

        def attn_switch(cidx, u_base, g_base, ngrp_max):
            engs = [nc.tensor, nc.vector, nc.scalar, nc.sync, nc.gpsimd]
            regs = [e.compute_val(e.partition_id() % 4) for e in engs]
            snap = dict(cnt)
            sgv = gv[0]
            bs = dict(snap)
            bs.update(U=u_base, G=g_base, gv=sgv)
            final = None
            for j in nc.Switch(engines=engs, index=regs, n=4):
                for k in cnt:
                    cnt[k] = snap[k]
                gv[0] = sgv
                cpos = (7 - j) if cidx == 0 else j
                ngrp = (8 - j) if cidx == 0 else (j + 1)
                attn_chunk(cpos, ngrp, bs, ngrp_max)
                final = (dict(cnt), gv[0])
            cnt.update(final[0])
            gv[0] = final[1]

        def emit_sink_vector(v, b_g8):
            v = DP(v)
            Z4 = skZ[:, :].rearrange("p (g i j) -> p g i j", g=4, i=4)
            E4 = skE[:, :].rearrange("p (g i j) -> p g i j", g=4, i=4)
            Exj = skE[:, :].rearrange("p (x j) -> p x j", j=2)
            M0a = skM[:, 0:32].rearrange("p (x o) -> p x o", o=1)
            M0xj = skM[:, 0:32].rearrange("p (x j) -> p x j", j=2)
            M1a = skM[:, 32:48].rearrange("p (x o) -> p x o", o=1)
            M0i = skM[:, 0:32].rearrange("p (g i j) -> p g i j", g=4, i=2)
            M0g = skM[:, 0:32].rearrange("p (g i j) -> p g i j", g=4, i=2)
            M1i = skM[:, 32:48].rearrange("p (g i j) -> p g i j", g=4, i=1)
            S1a = skM[:, 48:64].rearrange("p (x o) -> p x o", o=1)
            S1i = skM[:, 48:64].rearrange("p (g i j) -> p g i j", g=4, i=1)
            ub4 = skU[:, :].rearrange("p (g i o) -> p g i o", g=4, o=1) \
                .broadcast_to([128, 4, 4, 4])
            vb4 = skV[:, :].rearrange("p (g o j) -> p g o j", g=4, o=1) \
                .broadcast_to([128, 4, 4, 4])
            mr4 = skM[:, 32:48].rearrange("p (g i o) -> p g i o", g=4, o=1) \
                .broadcast_to([128, 4, 4, 4])
            mc4 = skM[:, 32:48].rearrange("p (g o j) -> p g o j", g=4, o=1) \
                .broadcast_to([128, 4, 4, 4])
            g8 = b_g8
            for it in range(SINK_IT):
                v.tensor_tensor(E4, Z4, vb4, ALU.add)
                v.tensor_tensor(M0a, Exj[:, :, 0:1], Exj[:, :, 1:2], ALU.max)
                v.tensor_tensor(M1a, M0xj[:, :, 0:1], M0xj[:, :, 1:2], ALU.max)
                v.tensor_sub(E4, E4, mr4)
                I(v.drain(), "g7", 1)
                g8 += 1
                W(v, "g8", g8)
                v.tensor_add(M0a, Exj[:, :, 0:1], Exj[:, :, 1:2])
                v.tensor_add(S1a, M0xj[:, :, 0:1], M0xj[:, :, 1:2])
                I(v.drain(), "g7", 1)
                g8 += 1
                W(v, "g8", g8)
                v.tensor_add(skU[:, :], skM[:, 32:48], skM[:, 16:32])
                v.tensor_scalar(skU[:, :], skU[:, :], -1.0, None, ALU.mult)
                v.tensor_tensor(E4, Z4, ub4, ALU.add)
                v.tensor_tensor(M0i, E4[:, :, 0:2, :], E4[:, :, 2:4, :], ALU.max)
                v.tensor_tensor(M1i, M0i[:, :, 0:1, :], M0i[:, :, 1:2, :],
                                ALU.max)
                v.tensor_sub(E4, E4, mc4)
                I(v.drain(), "g7", 1)
                g8 += 1
                W(v, "g8", g8)
                v.tensor_add(M0i, E4[:, :, 0:2, :], E4[:, :, 2:4, :])
                v.tensor_add(S1i, M0i[:, :, 0:1, :], M0i[:, :, 1:2, :])
                I(v.drain(), "g7", 1)
                g8 += 1
                W(v, "g8", g8)
                v.tensor_add(skV[:, :], skM[:, 32:48], skM[:, 16:32])
                v.tensor_scalar(skV[:, :], skV[:, :], -1.0, None, ALU.mult)
            v.tensor_tensor(E4, Z4, ub4, ALU.add)
            v.tensor_tensor(E4, E4, vb4, ALU.add)
            I(v.drain(), "g7", 1)
            g8 += 1
            W(v, "g8", g8)
            v.tensor_scalar(skZ[:, :], skZ[:, :], g0col[:, 0:1], None, ALU.mult)
            for dd in range(4):
                dv_ = skZ[:, :].rearrange("p (g c) -> p g c", g=4) \
                    [:, :, 5 * dd:5 * dd + 1]
                v.tensor_tensor(dv_, dv_,
                                g0ncol[:, 0:1].rearrange("p (o u) -> p o u", o=1)
                                .broadcast_to([128, 4, 1]), ALU.add)
            return g8

        def emit_sink_scalar(s, b_g7):
            g7 = b_g7
            for it in range(SINK_IT):
                for k in range(2):
                    g7 += 1
                    W(s, "g7", g7)
                    s.activation(skE[:, :], skE[:, :], AF.Exp)
                    I(s.drain(), "g8", 1)
                    g7 += 1
                    W(s, "g7", g7)
                    s.activation(skM[:, 16:32], skM[:, 48:64], AF.Ln)
                    I(s.drain(), "g8", 1)
            g7 += 1
            W(s, "g7", g7)
            s.activation(skZ[:, :], skE[:, :], AF.Exp)
            I(s.drain(), "g8", 1)
            return g7

        def softmax4(v, s_eng_queue, src, dst, tag):
            pass

        def combine(chunk):
            with nc.Block() as blk:
                b_g4 = cnt["g4"]

                @blk.vector
                def _(v):
                    v = DP(v)
                    v.tensor_copy(crowb[0:1, :], csb[0:1, :])
                    I(v.drain(), "g4", 1)
                    W(v, "g4", b_g4 + 2)
                    v.tensor_scalar(ncolT[:, chunk * 16:chunk * 16 + 16],
                                    ps[0][:, 0:16], -1.0, None, ALU.mult)
                    v.memset(csb[0:1, :], 0.0)

                @blk.tensor
                def _(t):
                    W(t, "g4", b_g4 + 1)
                    for h in range(4):
                        for qsub in range(4):
                            t.matmul(ps[0][:, h * 4 + qsub:h * 4 + qsub + 1],
                                     crowb[0:1, h * 512 + qsub * 128:
                                           h * 512 + qsub * 128 + 128],
                                     one1b[:, :], start=True, stop=True)
                    I(t.drain(), "g4", 1)

            with nc.Block() as blk:
                b_g1, b_g4, b_g5, b_g6 = cnt["g1"], cnt["g4"], cnt["g5"], cnt["g6"]

                @blk.sync
                def _(sy):
                    I(sy.dma_start(af(C_WST + 384, 768), wst), "g1", 16)
                    I(sy.dma_start(r24[0:1, :], bst[0:24]), "g1", 16)
                    W(sy, "g1", b_g1 + 32)
                    for i in range(8):
                        if i >= 2:
                            W(sy, "g5", b_g5 + i - 1)
                        I(sy.dma_start(
                            af(O_STV + (i % 2) * 2048, 2048)
                            .rearrange("p (a t) -> p a t", a=4),
                            xotr[i * 512:(i + 1) * 512,
                                 chunk * 512:(chunk + 1) * 512]
                            .rearrange("(a p) t -> p a t", p=128)), "g1", 16)

                @blk.vector
                def _(v):
                    W(v, "g1", b_g1 + 32)
                    v.tensor_copy(ab(C_WST, 768), af(C_WST + 384, 768))
                    I(v.drain(), "g6", 1)
                    for i in range(8):
                        W(v, "g1", b_g1 + 32 + 16 * min(i + 2, 8))
                        v.tensor_copy(ab(C_XTR + i * 1024, 2048),
                                      af(O_STV + (i % 2) * 2048, 2048))
                        I(v.drain(), "g5", 1)

                @blk.tensor
                def _(t):
                    W(t, "g6", b_g6 + 1)
                    if chunk == 0:
                        for c in range(32):
                            t.matmul(ps[2][0:1, 0:24], onescol[:, :],
                                     ab(C_WST + c * 12, 24),
                                     start=(c == 0), stop=(c == 31))
                        I(t.drain(), "g4", 1)
                        W(t, "g6", b_g6 + 2)
                        t.matmul(ps[3][:, 0:24], onesrow[:, :], rrow[0:1, 0:24],
                                 start=True, stop=True)
                        t.matmul(ps[2][:, 0:24], onesrow[:, :], r24[0:1, :],
                                 start=True, stop=True)
                        I(t.drain(), "g4", 1)
                    W(t, "g5", b_g5 + 8)
                    for tl in range(4):
                        for c in range(32):
                            t.matmul(ps[4 + (tl % 2)][:, 0:24],
                                     ab(C_XTR + (c * 512 + tl * 128) // 2, 128),
                                     ab(C_WST + c * 12, 24),
                                     start=(c == 0), stop=(c == 31))
                        I(t.drain(), "g4", 1)
                        if tl < 3:
                            W(t, "g6", b_g6 + (2 if chunk == 0 else 1) + tl)

                @blk.vector
                def _(v):
                    v = DP(v)
                    boff = 0
                    if chunk == 0:
                        W(v, "g4", b_g4 + 1)
                        v.tensor_copy(rrow[0:1, 0:24], ps[2][0:1, 0:24])
                        I(v.drain(), "g6", 1)
                        W(v, "g4", b_g4 + 2)
                        v.tensor_copy(wcolrep[:, :], ps[3][:, 0:24])
                        v.tensor_copy(biasrep[:, :], ps[2][:, 0:24])
                        boff = 2
                    for tl in range(4):
                        i = chunk * 4 + tl
                        W(v, "g4", b_g4 + boff + tl + 1)
                        v.tensor_scalar(lsb[:, tl * 24:(tl + 1) * 24],
                                        ps[4 + (tl % 2)][:, 0:24],
                                        own_rrms[:, i:i + 1], None, ALU.mult)
                        v.scalar_tensor_tensor(lsb[:, tl * 24:(tl + 1) * 24],
                                               wcolrep[:, :],
                                               own_nmrr[:, i:i + 1],
                                               lsb[:, tl * 24:(tl + 1) * 24],
                                               ALU.mult, ALU.add)
                        v.tensor_add(lsb[:, tl * 24:(tl + 1) * 24],
                                     lsb[:, tl * 24:(tl + 1) * 24], biasrep[:, :])
                        I(v.drain(), "g6", 1)

            with nc.Block() as blk:
                b_g7, b_g8 = cnt["g7"], cnt["g8"]
                lv = lsb[:, :].rearrange("p (g c) -> p g c", g=4)

                @blk.vector
                def _(v):
                    v = DP(v)
                    v.tensor_scalar(skZ[:, :].rearrange("p (g c) -> p g c", g=4),
                                    lv[:, :, 0:16],
                                    TAU_INV, None, ALU.mult)
                    v.memset(skU[:, :], 0.0)
                    v.memset(skV[:, :], 0.0)
                    g8 = emit_sink_vector(v, b_g8)
                    for kk, dst in ((16, hpre), (20, hpost)):
                        src = lv[:, :, kk:kk + 4]
                        m2 = skM[:, 0:8].rearrange("p (g c) -> p g c", g=4)
                        m1 = skM[:, 8:12].rearrange("p (g o) -> p g o", g=4)
                        d3 = dst[:, :].rearrange("p (g c) -> p g c", g=4)
                        v.tensor_tensor(m2, src[:, :, 0:2], src[:, :, 2:4],
                                        ALU.max)
                        v.tensor_tensor(m1, m2[:, :, 0:1], m2[:, :, 1:2],
                                        ALU.max)
                        v.tensor_sub(d3, src, m1.broadcast_to([128, 4, 4]))
                        I(v.drain(), "g7", 1)
                        g8 += 1
                        W(v, "g8", g8)
                        v.tensor_add(m2, d3[:, :, 0:2], d3[:, :, 2:4])
                        v.tensor_add(m1, m2[:, :, 0:1], m2[:, :, 1:2])
                        v.reciprocal(skM[:, 12:16], skM[:, 8:12])
                        r1 = skM[:, 12:16].rearrange("p (g o) -> p g o", g=4)
                        v.tensor_tensor(d3, d3, r1.broadcast_to([128, 4, 4]),
                                        ALU.mult)
                    I(v.drain(), "g6", 1)

                @blk.scalar
                def _(s):
                    g7 = emit_sink_scalar(s, b_g7)
                    for kk, dst in ((16, hpre), (20, hpost)):
                        g7 += 1
                        W(s, "g7", g7)
                        s.activation(dst[:, :], dst[:, :], AF.Exp)
                        I(s.drain(), "g8", 1)

            with nc.Block() as blk:
                b_dv = cnt["dv"]
                b_g6 = cnt["g6"]

                @blk.sync
                def _(sy):
                    for tl in range(4):
                        W(sy, "vb", 2 * gv[0])
                        I(sy.dma_start(
                            af(O_STV, 4096),
                            xo[(chunk * 4 + tl) * 128:(chunk * 4 + tl + 1) * 128, :]),
                            "dv", 16)
                        gv[0] += 1

                @blk.vector
                def _(v):
                    v = DP(v)
                    for tl in range(4):
                        i = chunk * 4 + tl
                        W(v, "dv", b_dv + 16 * (tl + 1))
                        v.tensor_scalar(ab(C_ATL, 2048), af(O_STV, 2048),
                                        own_nm[:, i:i + 1], own_r[:, i:i + 1],
                                        ALU.add, ALU.mult)
                        v.tensor_scalar(ab(C_ATL + 1024, 2048),
                                        af(O_STV + 2048, 2048),
                                        own_nm[:, i:i + 1], own_r[:, i:i + 1],
                                        ALU.add, ALU.mult)
                        I(v.drain(), "vb", 2)
                        xp = af(C_XPR + tl * 1024, 1024)
                        v.tensor_scalar(xp, ab(C_ATL, 1024),
                                        hpre[:, tl * 4:tl * 4 + 1], None, ALU.mult)
                        for n in range(1, 4):
                            v.scalar_tensor_tensor(
                                xp, ab(C_ATL + n * 512, 1024),
                                hpre[:, tl * 4 + n:tl * 4 + n + 1],
                                xp, ALU.mult, ALU.add)
                        v.tensor_copy(ab(C_XPT + tl * 512, 1024), xp)
                        I(v.drain(), "g6", 1)

                b_qt = cnt["qt"]

                @blk.tensor
                def _(t):
                    for tl in range(4):
                        W(t, "g6", b_g6 + tl + 1)
                        for c8 in range(2):
                            if tl * 2 + c8 >= 2:
                                W(t, "qe", cnt["qe"] + tl * 2 + c8 - 1)
                            pv = ps[6 + ((tl * 2 + c8) % 2)][:, :].bitcast(BF16)
                            for cc in range(4):
                                c = c8 * 4 + cc
                                t.transpose(pv[:, cc * 128:(cc + 1) * 128],
                                            ab(C_XPT + tl * 512 + c * 64, 128),
                                            ident[:, :])
                            I(t.drain(), "qt", 1)

                @blk.vector
                def _(v):
                    for tl in range(4):
                        for c8 in range(2):
                            W(v, "qt", b_qt + tl * 2 + c8 + 1)
                            pv = ps[6 + ((tl * 2 + c8) % 2)][:, :].bitcast(BF16)
                            for cc in range(4):
                                c = c8 * 4 + cc
                                v.tensor_copy(
                                    ab(C_YT + (c * 512 + tl * 128) // 2, 128),
                                    pv[:, cc * 128:(cc + 1) * 128])
                            I(v.drain(), "qe", 1)

            with nc.Block() as blk:
                b_g1, b_g4, b_g5 = cnt["g1"], cnt["g4"], cnt["g5"]

                @blk.sync
                def _(sy):
                    for i in range(8):
                        W(sy, "g5", b_g5 + i)
                        I(sy.dma_start(rrow[0:1, :], wsc[1:2, i * 512:(i + 1) * 512]),
                          "g1", 16)

                @blk.tensor
                def _(t):
                    for i in range(8):
                        W(t, "g1", b_g1 + 16 * (i + 1))
                        t.matmul(ps[1][:, :], onesrow[:, :], rrow[0:1, :],
                                 start=True, stop=True)
                        I(t.drain(), "g4", 1)

                @blk.vector
                def _(v):
                    for i in range(8):
                        W(v, "g4", b_g4 + i + 1)
                        v.tensor_copy(af(O_XVB + i * 512, 512), ps[1][:, :])
                        I(v.drain(), "g5", 1)

            with nc.Block() as blk:
                b_dk = cnt["dk"]
                b_g5, b_g4, b_g6 = cnt["g5"], cnt["g4"], cnt["g6"]
                b_qt = cnt["qt"]

                @blk.sync
                def _(sy):
                    for ci in range(8):
                        if ci >= 2:
                            W(sy, "g5", b_g5 + ci - 1)
                        I(sy.dma_start(af(O_STV + (ci % 2) * 1024, 1024),
                                       wbr[ci]), "dk", 16)

                b_kb = cnt["kb"]

                @blk.vector
                def _(v):
                    for ci in range(8):
                        if ci >= 2:
                            W(v, "kb", b_kb + ci - 1)
                        W(v, "dk", b_dk + 16 * min(ci + 2, 8))
                        v.tensor_copy(ab(C_WBB + (ci % 2) * 512, 1024),
                                      af(O_STV + (ci % 2) * 1024, 1024))
                        I(v.drain(), "g5", 1)

                @blk.tensor
                def _(t):
                    for ci in range(8):
                        W(t, "g5", b_g5 + ci + 1)
                        for co in range(8):
                            t.matmul(ps[co][:, :],
                                     ab(C_WBB + (ci % 2) * 512 + co * 64, 128),
                                     ab(C_YT + ci * 256, 512),
                                     start=(ci == 0), stop=(ci == 7))
                        I(t.drain(), "kb", 1)
                    I(t.drain(), "g4", 1)

                @blk.vector
                def _(v):
                    W(v, "g4", b_g4 + 1)
                    for co in range(8):
                        v.tensor_copy(ab(C_XPT + co * 256, 512), ps[co][:, :])
                        I(v.drain(), "g6", 1)

                @blk.tensor
                def _(t):
                    W(t, "g6", b_g6 + 8)
                    for co in range(8):
                        if co >= 2:
                            W(t, "qe", cnt["qe"] + co - 1)
                        pv = ps[6 + (co % 2)][:, :].bitcast(BF16)
                        for tl in range(4):
                            t.transpose(pv[:, tl * 128:(tl + 1) * 128],
                                        ab(C_XPT + co * 256 + tl * 64, 128),
                                        ident[:, :])
                        I(t.drain(), "qt", 1)

                @blk.vector
                def _(v):
                    for co in range(8):
                        W(v, "qt", b_qt + co + 1)
                        pv = ps[6 + (co % 2)][:, :].bitcast(BF16)
                        for tl in range(4):
                            v.tensor_copy(ab(C_Y + tl * 512 + co * 64, 128),
                                          pv[:, tl * 128:(tl + 1) * 128])
                        I(v.drain(), "qe", 1)

            with nc.Block() as blk:
                b_dv = cnt["dv"]
                b_g4, b_g5 = cnt["g4"], cnt["g5"]
                b_sq, b_va, b_sd = cnt["sq"], cnt["va"], cnt["sd"]
                b_qt, b_qe = cnt["qt"], cnt["qe"]
                b_g1, b_g2 = cnt["g1"], cnt["g2"]

                @blk.sync
                def _(sy):
                    for tl in range(4):
                        W(sy, "vb", 2 * gv[0])
                        I(sy.dma_start(
                            af(O_STV, 4096),
                            xo[(chunk * 4 + tl) * 128:(chunk * 4 + tl + 1) * 128, :]),
                            "dv", 16)
                        gv[0] += 1
                        W(sy, "g4", b_g4 + tl + 1)
                        row = chunk * 512 + tl * 128
                        I(sy.dma_start(x1s[row:row + 128, :], af(C_XF, 4096)),
                          "g1", 16)
                        for c4 in range(8):
                            W(sy, "qe", b_qe + tl * 8 + c4 + 1)
                            tg = chunk * 4 + tl
                            I(sy.dma_start(
                                xn1ts[:, :].rearrange("p (c t) -> p c t", c=32)
                                [:, 4 * c4:4 * c4 + 4, tg * 128:(tg + 1) * 128],
                                ab(C_XPT + c4 * 256, 512)
                                .rearrange("p (c t) -> p c t", c=4)), "g2", 16)

                @blk.scalar
                def _(s):
                    for tl in range(4):
                        if tl > 0:
                            W(s, "va", b_va + tl)
                        W(s, "g4", b_g4 + tl + 1)
                        for h4 in range(4):
                            if h4:
                                s.drain()
                            s.activation(ab(O_JNK, 1024),
                                         af(C_XF + h4 * 1024, 1024),
                                         AF.Square, accum_out=af(O_SQ + h4, 1))
                        I(s.drain(), "sq", 1)
                        W(s, "va", b_va + tl + 1)
                        emit_sqrt_scalar(s, tl % 2)

                @blk.vector
                def _(v):
                    v = DP(v)
                    for tl in range(4):
                        i = chunk * 4 + tl
                        W(v, "dv", b_dv + 16 * (tl + 1))
                        if tl >= 1:
                            W(v, "g1", b_g1 + 16 * tl)
                        v.tensor_scalar(ab(C_ATL, 2048), af(O_STV, 2048),
                                        own_nm[:, i:i + 1], own_r[:, i:i + 1],
                                        ALU.add, ALU.mult)
                        v.tensor_scalar(ab(C_ATL + 1024, 2048),
                                        af(O_STV + 2048, 2048),
                                        own_nm[:, i:i + 1], own_r[:, i:i + 1],
                                        ALU.add, ALU.mult)
                        for q in range(4):
                            xq = af(C_XF + q * 1024, 1024)
                            v.scalar_tensor_tensor(
                                xq, ab(O_ACC + (tl * 4096 + q * 1024) // 2, 1024),
                                ncolT[:, chunk * 16 + q * 4 + tl:
                                      chunk * 16 + q * 4 + tl + 1],
                                af(O_XVB + q * 1024, 1024), ALU.add, ALU.mult)
                            for jj in range(4):
                                v.scalar_tensor_tensor(
                                    xq, ab(C_ATL + jj * 512, 1024),
                                    skZ[:, tl * 16 + q * 4 + jj:
                                        tl * 16 + q * 4 + jj + 1],
                                    xq, ALU.mult, ALU.add)
                            v.scalar_tensor_tensor(
                                xq, ab(C_Y + tl * 512, 1024),
                                hpost[:, tl * 4 + q:tl * 4 + q + 1],
                                xq, ALU.mult, ALU.add)
                            v.tensor_add(xq, xq, af(O_STV + q * 1024, 1024))
                        I(v.drain(), "vb", 2)
                        I(v.drain(), "g4", 1)
                        W(v, "sq", b_sq + tl + 1)
                        v.reduce_sum(sm[:, 2:3], af(C_XF, 4096), axis=AX.X)
                        v.tensor_add(af(O_SQ, 1), af(O_SQ, 1), af(O_SQ + 1, 1))
                        v.tensor_add(af(O_SQ + 2, 1), af(O_SQ + 2, 1),
                                     af(O_SQ + 3, 1))
                        v.tensor_add(af(O_SQ, 1), af(O_SQ, 1), af(O_SQ + 2, 1))
                        v.tensor_scalar(sm[:, 2:3], sm[:, 2:3], 1.0 / D, None,
                                        ALU.mult)
                        v.tensor_scalar(af(O_SQ, 1), af(O_SQ, 1), 1.0 / D, None,
                                        ALU.mult)
                        v.tensor_tensor(sm[:, 3:4], sm[:, 2:3], sm[:, 2:3],
                                        ALU.mult)
                        v.tensor_sub(af(O_SQ + 4 + tl % 2, 1), af(O_SQ, 1),
                                     sm[:, 3:4])
                        v.tensor_scalar(af(O_SQ + 4 + tl % 2, 1),
                                        af(O_SQ + 4 + tl % 2, 1), 1.0, EPS_LN,
                                        ALU.mult, ALU.add)
                        I(v.drain(), "va", 1)
                        v.tensor_scalar(o1_nm[:, i:i + 1], sm[:, 2:3], -1.0,
                                        None, ALU.mult)
                        W(v, "sd", b_sd + tl + 1)
                        v.reciprocal(o1_r[:, i:i + 1], af(O_SQ + 6 + tl % 2, 1))
                        newton(v, o1_r[:, i:i + 1], af(O_SQ + 4 + tl % 2, 1),
                               sm[:, 4:5])
                        v.tensor_scalar(ab(C_XN1, 2048), af(C_XF, 2048),
                                        o1_nm[:, i:i + 1], o1_r[:, i:i + 1],
                                        ALU.add, ALU.mult)
                        v.tensor_scalar(ab(C_XN1 + 1024, 2048),
                                        af(C_XF + 2048, 2048),
                                        o1_nm[:, i:i + 1], o1_r[:, i:i + 1],
                                        ALU.add, ALU.mult)
                        I(v.drain(), "g5", 1)
                        for c4 in range(8):
                            W(v, "qt", b_qt + tl * 8 + c4 + 1)
                            pv = ps[6 + (c4 % 2)][:, :].bitcast(BF16)
                            if tl * 8 + c4 >= 8:
                                W(v, "g2", b_g2 + 16 * (tl * 8 + c4))
                            v.tensor_copy(ab(C_XPT + c4 * 256, 512)
                                          if False else
                                          ab(C_XPT + c4 * 256, 512), pv[:, 0:512])
                            I(v.drain(), "qe", 1)

                @blk.tensor
                def _(t):
                    for tl in range(4):
                        W(t, "g5", b_g5 + tl + 1)
                        for c4 in range(8):
                            if tl * 8 + c4 >= 2:
                                W(t, "qe", b_qe + tl * 8 + c4 - 1)
                            pv = ps[6 + (c4 % 2)][:, :].bitcast(BF16)
                            for cc in range(4):
                                c = c4 * 4 + cc
                                t.transpose(pv[:, cc * 128:(cc + 1) * 128],
                                            ab(C_XN1 + c * 64, 128), ident[:, :])
                            I(t.drain(), "qt", 1)

        def fence(sems_to_wait):
            with nc.Block() as blk:
                @blk.gpsimd
                def _(g):
                    for s in sems_to_wait:
                        W(g, s, cnt[s])

        if stage >= 1:
            build_qw(0)
        if stage >= 2:
            attn_switch(0, 0, 0, 8)
            fence(["dv", "dk", "ev", "cv", "at", "av", "sg", "ts", "vb", "kb"])
        if stage >= 3:
            combine(0)
        if stage >= 4:
            fence(["g1", "g2"])
            build_qw(1)
            attn_switch(1, 32, 8, 4)
            fence(["dv", "dk", "ev", "cv", "at", "av", "sg", "ts", "vb", "kb"])
            combine(1)
        if stage <= 4:
            fence(["dv", "dk", "g1", "g2", "qe", "ou", "sc", "av", "ev"])
            return finish_early()
        fence(["g1", "g2", "dv", "qe"])

        with nc.Block() as blk:
            b_g1, b_g2 = cnt["g1"], cnt["g2"]
            b_g3, b_g4, b_g5 = cnt["g3"], cnt["g4"], cnt["g5"]
            b_dv = cnt["dv"]

            b_qg = cnt["qg"]

            @blk.gpsimd
            def _(g):
                for h in range(32):
                    if h >= 1:
                        W(g, "qg", b_qg + h)
                    I(g.dma_start(af(L_CC, 4096), wqh[h]), "gq", 16)
                W(g, "qg", b_qg + 32)
                for q in range(4):
                    I(g.dma_start(ab(L_XNT + q * 4096, 8192),
                                  xn1ts[:, q * 8192:(q + 1) * 8192]), "gq", 16)

            @blk.vector
            def _(v):
                v = DP(v)
                for h in range(32):
                    W(v, "gq", 16 * (h + 1))
                    if h == 0:
                        v.tensor_copy(af(L_WQ, 4096), af(L_CC, 4096))
                    else:
                        v.tensor_add(af(L_WQ, 4096), af(L_WQ, 4096),
                                     af(L_CC, 4096))
                    I(v.drain(), "qg", 1)

            @blk.sync
            def _(sy):
                I(sy.dma_start(af(L_HT, 4096), wkp), "dv", 16)
                W(sy, "g4", b_g4 + 1)
                I(sy.dma_start(af(L_HT, 4096), wvp), "dv", 16)
                W(sy, "g4", b_g4 + 2)
                I(sy.dma_start(af(L_HT, 2048), wrp), "dv", 16)
                W(sy, "g4", b_g4 + 3)
                I(sy.dma_start(af(L_HT, 32), bq), "dv", 16)
                I(sy.dma_start(rrow[0:1, 0:128], bkv[0:1, :]), "dv", 16)
                I(sy.dma_start(rrow[0:1, 128:256], bkv[1:2, :]), "dv", 16)

            @blk.vector
            def _(v):
                v = DP(v)
                W(v, "dv", b_dv + 16)
                v.tensor_copy(ab(L_WKB, 4096), af(L_HT, 4096))
                I(v.drain(), "g4", 1)
                W(v, "dv", b_dv + 32)
                v.tensor_copy(ab(L_WVB, 4096), af(L_HT, 4096))
                I(v.drain(), "g4", 1)
                W(v, "dv", b_dv + 48)
                v.tensor_copy(ab(L_WRB, 2048), af(L_HT, 2048))
                I(v.drain(), "g4", 1)
                W(v, "dv", b_dv + 96)
                v.reduce_sum(sm[:, 0:1], af(L_HT, 32), axis=AX.X)
                v.tensor_copy(bqcol_b[:, 0:1], sm[:, 0:1])
                v.tensor_copy(bkrow[0:1, :], rrow[0:1, 0:128])
                v.tensor_copy(bvrow[0:1, :], rrow[0:1, 128:256])
                I(v.drain(), "g5", 1)
                W(v, "g4", b_g4 + 4)
                v.tensor_copy(bqrow[0:1, :], ps[0][0:1, 0:128])
                W(v, "gq", 16 * 36)
                v.tensor_copy(ab(L_WQB, 4096), af(L_WQ, 4096))
                I(v.drain(), "g5", 1)

            @blk.tensor
            def _(t):
                W(t, "g5", b_g5 + 1)
                t.matmul(ps[0][0:1, 0:128], bqcol_b[:, 0:1], ident[:, :],
                         start=True, stop=True)
                I(t.drain(), "g4", 1)

        with nc.Block() as blk:
            b_g4, b_g5, b_g6, b_g7, b_g8 = (cnt["g4"], cnt["g5"], cnt["g6"],
                                            cnt["g7"], cnt["g8"])

            @blk.tensor
            def _(t):
                for i in range(8):
                    for kind in range(4):
                        n = 4 * i + kind
                        bank = n % 2
                        wq_off = [L_WQB, L_WKB, L_WVB, L_WRB][kind]
                        nw = [64, 64, 64, 32][kind]
                        ncols = [128, 128, 128, 64][kind]
                        if n >= 2:
                            np_ = n - 2
                            W(t, "g5", b_g5 + 5 * (np_ // 4) + (np_ % 4) + 1)
                        for c in range(32):
                            t.matmul(ps[bank][:, 0:ncols],
                                     ab(L_XNT + c * 512 + i * 64, 128),
                                     ab(wq_off + c * nw, ncols),
                                     start=(c == 0),
                                     stop=(kind == 3 and c == 31))
                        if kind == 0:
                            t.matmul(ps[bank][:, 0:128], onesrowb[:, :],
                                     bqrow[0:1, :], start=False, stop=True)
                        elif kind == 1:
                            t.matmul(ps[bank][:, 0:128], onesrowb[:, :],
                                     bkrow[0:1, :], start=False, stop=True)
                        elif kind == 2:
                            t.matmul(ps[bank][:, 0:128], onesrowb[:, :],
                                     bvrow[0:1, :], start=False, stop=True)
                        I(t.drain(), "g4", 1)
                    W(t, "g6", b_g6 + i + 1)
                    if i >= 1:
                        W(t, "g5", b_g5 + 5 * (i - 1) + 5)
                    pv2 = ps[2][:, :].bitcast(BF16)
                    t.transpose(pv2[0:64, 0:128], gb_tile[:, 0:64], ident[:, :])
                    I(t.drain(), "g4", 1)

            @blk.vector
            def _(v):
                v = DP(v)
                for i in range(8):
                    for kind in range(4):
                        W(v, "g4", b_g4 + i * 5 + kind + 1)
                        bank = (4 * i + kind) % 2
                        if kind == 0:
                            v.tensor_scalar(af(L_CC + i * 128, 128),
                                            ps[bank][:, 0:128], SC1, None,
                                            ALU.mult)
                        elif kind == 1:
                            v.tensor_copy(af(L_K + i * 128, 128),
                                          ps[bank][:, 0:128])
                        elif kind == 2:
                            v.tensor_copy(af(L_VF + i * 128, 128),
                                          ps[bank][:, 0:128])
                            v.tensor_copy(ab(L_VB + i * 64, 128),
                                          af(L_VF + i * 128, 128))
                        else:
                            v.tensor_copy(af(L_SCO, 64), ps[bank][:, 0:64])
                        I(v.drain(), "g5", 1)
                    sco = af(L_SCO, 64)
                    v.max(sm[:, 0:8], sco)
                    v.tensor_scalar(af(L_SCO + 64, 64), sco, sm[:, 7:8], None,
                                    ALU.is_ge)
                    v.scalar_tensor_tensor(af(L_SCO + 128, 64),
                                           af(L_SCO + 64, 64), NEG_BIG, sco,
                                           ALU.mult, ALU.add)
                    v.max(sm[:, 8:16], af(L_SCO + 128, 64))
                    v.tensor_scalar(sm[:, 16:17], sm[:, 0:1], -1.0, None,
                                    ALU.mult)
                    v.tensor_scalar(af(L_SCO + 192, 64), sco, sm[:, 16:17],
                                    None, ALU.add)
                    I(v.drain(), "g7", 1)
                    W(v, "g8", b_g8 + i + 1)
                    v.tensor_scalar(af(L_SCO + 64, 64), sco, sm[:, 15:16], None,
                                    ALU.is_ge)
                    v.tensor_tensor(af(L_SCO + 128, 64), af(L_SCO + 192, 64),
                                    af(L_SCO + 64, 64), ALU.mult)
                    v.reduce_sum(sm[:, 17:18], af(L_SCO + 128, 64), axis=AX.X)
                    v.reciprocal(sm[:, 18:19], sm[:, 17:18])
                    v.tensor_scalar(af(L_G + i * 64, 64), af(L_SCO + 128, 64),
                                    sm[:, 18:19], SCALING, ALU.mult, ALU.mult)
                    v.tensor_copy(gb_tile[:, 0:64], af(L_G + i * 64, 64))
                    I(v.drain(), "g6", 1)
                    W(v, "g4", b_g4 + i * 5 + 5)
                    pv2 = ps[2][:, :].bitcast(BF16)
                    v.tensor_copy(gtsb[:, i * 128:(i + 1) * 128],
                                  pv2[0:64, 0:128])
                    I(v.drain(), "g5", 1)

            @blk.scalar
            def _(s):
                for i in range(8):
                    W(s, "g7", b_g7 + i + 1)
                    s.activation(af(L_SCO + 192, 64), af(L_SCO + 192, 64),
                                 AF.Exp)
                    I(s.drain(), "g8", 1)

        with nc.Block() as blk:
            b_g7, b_g8, b_g6 = cnt["g7"], cnt["g8"], cnt["g6"]
            b_qt, b_qe = cnt["qt"], cnt["qe"]

            @blk.vector
            def _(v):
                v = DP(v)
                g8 = b_g8
                for i in range(8):
                    kt = af(L_K + i * 128, 128)
                    v.reduce_max(sm[:, 0:1], kt, axis=AX.X)
                    v.tensor_reduce(sm[:, 1:2], kt, AX.X, ALU.min)
                    cct = af(L_CC + i * 128, 128)
                    v.reduce_max(sm[:, 2:3], cct, axis=AX.X)
                    v.tensor_reduce(sm[:, 3:4], cct, AX.X, ALU.min)
                    v.tensor_tensor(sm[:, 4:5], sm[:, 0:1], sm[:, 2:3], ALU.mult)
                    v.tensor_tensor(sm[:, 5:6], sm[:, 0:1], sm[:, 3:4], ALU.mult)
                    v.tensor_tensor(sm[:, 6:7], sm[:, 1:2], sm[:, 2:3], ALU.mult)
                    v.tensor_tensor(sm[:, 7:8], sm[:, 1:2], sm[:, 3:4], ALU.mult)
                    v.tensor_tensor(sm[:, 4:5], sm[:, 4:5], sm[:, 5:6], ALU.max)
                    v.tensor_tensor(sm[:, 6:7], sm[:, 6:7], sm[:, 7:8], ALU.max)
                    v.tensor_tensor(sm[:, 4:5], sm[:, 4:5], sm[:, 6:7], ALU.max)
                    v.tensor_scalar(sm[:, 9:10], sm[:, 4:5], -1.0, None, ALU.mult)
                    for db in range(8):
                        for d in range(16):
                            cc1 = arena[:, L_CC + i * 128 + db * 16 + d:
                                        L_CC + i * 128 + db * 16 + d + 1]
                            v.tensor_scalar(af(L_ARG + d * 128, 128), kt, cc1,
                                            sm[:, 9:10], ALU.mult, ALU.add)
                        I(v.drain(), "g7", 1)
                        g8 += 1
                        W(v, "g8", g8)
                        v.reduce_sum(arena[:, L_DEN + i * 128 + db * 16:
                                           L_DEN + i * 128 + db * 16 + 16],
                                     ab(L_EB, 2048)
                                     .rearrange("p (d e) -> p d e", d=16),
                                     axis=AX.X)
                        for d in range(16):
                            esl = ab(L_EB + d * 64, 128)
                            v.scalar_tensor_tensor(
                                esl, esl, 1.0, ab(L_VB + i * 64, 128),
                                ALU.mult, ALU.mult,
                                accum_out=arena[:, L_NUM + i * 128 + db * 16 + d:
                                                L_NUM + i * 128 + db * 16 + d + 1])
                    v.reciprocal(af(L_ARG, 128), af(L_DEN + i * 128, 128))
                    v.tensor_tensor(af(L_OO + i * 128, 128),
                                    af(L_NUM + i * 128, 128), af(L_ARG, 128),
                                    ALU.mult)
                    v.tensor_copy(ab(L_SCO + i * 64, 128), af(L_OO + i * 128, 128))
                    I(v.drain(), "g6", 1)

            @blk.scalar
            def _(s):
                g7 = b_g7
                for i in range(8):
                    for db in range(8):
                        g7 += 1
                        W(s, "g7", g7)
                        s.activation(ab(L_EB, 2048), af(L_ARG, 2048), AF.Exp)
                        I(s.drain(), "g8", 1)

            @blk.tensor
            def _(t):
                for i in range(8):
                    W(t, "g6", b_g6 + i + 1)
                    for dt in range(8):
                        idx = i * 8 + dt
                        if idx >= 2:
                            W(t, "qe", b_qe + idx - 1)
                        pv = ps[6 + (dt % 2)][:, :].bitcast(BF16)
                        t.transpose(pv[0:16, 0:128],
                                    ab(L_SCO + i * 64 + dt * 8, 16), ident[:, :])
                        I(t.drain(), "qt", 1)

            @blk.vector
            def _(v):
                for i in range(8):
                    for dt in range(8):
                        W(v, "qt", b_qt + i * 8 + dt + 1)
                        pv = ps[6 + (dt % 2)][:, :].bitcast(BF16)
                        v.tensor_copy(otsb[:, (dt * 8 + i) * 128:
                                           (dt * 8 + i) * 128 + 128],
                                      pv[0:16, 0:128])
                        I(v.drain(), "qe", 1)

        with nc.Block() as blk:
            b_dv, b_g4, b_g5, b_g7 = cnt["dv"], cnt["g4"], cnt["g5"], cnt["g7"]
            b_g6 = cnt["g6"]

            @blk.sync
            def _(sy):
                for nt in range(2):
                    for c in range(32):
                        idx = nt * 32 + c
                        if idx >= 2:
                            W(sy, "g5", b_g5 + idx - 1)
                        I(sy.dma_start(af(L_WQ + (idx % 2) * 1024, 1024),
                                       at_[c]), "dv", 16)

            @blk.vector
            def _(v):
                for nt in range(2):
                    for c in range(32):
                        idx = nt * 32 + c
                        W(v, "dv", b_dv + 16 * min(idx + 2, 64))
                        if idx >= 2:
                            W(v, "g7", b_g7 + idx - 1)
                        v.tensor_copy(ab(L_ATB + (idx % 2) * 512, 1024),
                                      af(L_WQ + (idx % 2) * 1024, 1024))
                        I(v.drain(), "g5", 1)
                    W(v, "g4", b_g4 + nt + 1)
                    for sk in range(8):
                        v.tensor_copy(ab(L_HT + sk * 512 + nt * 256, 512),
                                      ps[sk][:, :])
                    I(v.drain(), "g6", 1)

            @blk.tensor
            def _(t):
                for nt in range(2):
                    if nt == 1:
                        W(t, "g6", b_g6 + 1)
                    for c in range(32):
                        idx = nt * 32 + c
                        W(t, "g5", b_g5 + idx + 1)
                        for sk in range(8):
                            t.matmul(ps[sk][:, :],
                                     ab(L_ATB + (idx % 2) * 512 + sk * 64, 128),
                                     ab(L_XNT + c * 512 + nt * 256, 512),
                                     start=(c == 0), stop=(c == 31))
                        I(t.drain(), "g7", 1)
                    I(t.drain(), "g4", 1)

        with nc.Block() as blk:
            b_g4, b_g5 = cnt["g4"], cnt["g5"]

            @blk.tensor
            def _(t):
                for nt in range(2):
                    for sk in range(8):
                        idx = nt * 8 + sk
                        if idx >= 2:
                            W(t, "g5", b_g5 + idx - 1)
                        t.matmul(ps[idx % 2][:, :], repG[:, sk * 128:(sk + 1) * 128],
                                 gtsb[:, nt * 512:(nt + 1) * 512],
                                 start=True, stop=True)
                        I(t.drain(), "g4", 1)

            @blk.vector
            def _(v):
                for nt in range(2):
                    for sk in range(8):
                        idx = nt * 8 + sk
                        W(v, "g4", b_g4 + idx + 1)
                        sl = ab(L_HT + sk * 512 + nt * 256, 512)
                        v.tensor_tensor(sl, sl, ps[idx % 2][:, :], ALU.mult)
                        I(v.drain(), "g5", 1)

        with nc.Block() as blk:
            b_dv, b_dk = cnt["dv"], cnt["dk"]
            b_g4, b_g5, b_g6, b_g7, b_g8 = (cnt["g4"], cnt["g5"], cnt["g6"],
                                            cnt["g7"], cnt["g8"])
            b_ou = cnt["ou"]

            @blk.sync
            def _(sy):
                for dt in range(8):
                    for sk in range(8):
                        idx = dt * 8 + sk
                        if idx >= 2:
                            W(sy, "g5", b_g5 + idx - 1)
                        I(sy.dma_start(af(L_WQ + (idx % 2) * 512, 512),
                                       bt_[sk * 128:(sk + 1) * 128,
                                           dt * 512:(dt + 1) * 512]), "dv", 16)
                    for nb in range(8):
                        idx2 = dt * 8 + nb
                        if idx2 >= 2:
                            W(sy, "g6", b_g6 + idx2 - 1)
                        I(sy.dma_start(af(L_X1T + (idx2 % 2) * 512, 512),
                                       x1s[nb * 128:(nb + 1) * 128,
                                           dt * 512:(dt + 1) * 512]), "dk", 16)

            @blk.gpsimd
            def _(g):
                for dt in range(8):
                    for nb in range(8):
                        idx2 = dt * 8 + nb
                        W(g, "g8", b_g8 + idx2 + 1)
                        I(g.dma_start(out[nb * 128:(nb + 1) * 128,
                                          dt * 512:(dt + 1) * 512],
                                      af(L_OSB + (idx2 % 2) * 512, 512)),
                          "ou", 16)

            @blk.vector
            def _(v):
                for dt in range(8):
                    for sk in range(8):
                        idx = dt * 8 + sk
                        W(v, "dv", b_dv + 16 * (idx + (2 if sk < 7 else 1)))
                        if idx >= 2:
                            W(v, "g7", b_g7 + idx - 1)
                        v.tensor_copy(ab(L_BTB + (idx % 2) * 256, 512),
                                      af(L_WQ + (idx % 2) * 512, 512))
                        I(v.drain(), "g5", 1)
                    for nb in range(8):
                        idx2 = dt * 8 + nb
                        W(v, "g4", b_g4 + dt + 1)
                        W(v, "dk", b_dk + 16 * (idx2 + (2 if nb < 7 else 1)))
                        if idx2 >= 2:
                            W(v, "ou", b_ou + 16 * idx2)
                        v.tensor_add(af(L_OSB + (idx2 % 2) * 512, 512),
                                     ps[nb][:, :],
                                     af(L_X1T + (idx2 % 2) * 512, 512))
                        I(v.drain(), "g6", 1)
                        I(v.drain(), "g8", 1)

            @blk.tensor
            def _(t):
                for dt in range(8):
                    if dt >= 1:
                        W(t, "g6", b_g6 + dt * 8)
                    for sk in range(8):
                        idx = dt * 8 + sk
                        W(t, "g5", b_g5 + idx + 1)
                        for nb in range(8):
                            t.matmul(ps[nb][:, :],
                                     ab(L_HT + sk * 512 + nb * 64, 128),
                                     ab(L_BTB + (idx % 2) * 256, 512),
                                     start=(sk == 0), stop=False)
                        I(t.drain(), "g7", 1)
                    for nb in range(8):
                        t.matmul(ps[nb][:, :],
                                 otsb[:, (dt * 8 + nb) * 128:
                                      (dt * 8 + nb) * 128 + 128],
                                 rep16[:, :], start=False, stop=True)
                    I(t.drain(), "g4", 1)

        with nc.Block() as blk:
            @blk.gpsimd
            def _(g):
                W(g, "ou", cnt["ou"])

    return nc


def _host_maps(inputs):
    asc = np.ascontiguousarray
    x = np.asarray(inputs["L0_attn_x"], dtype=np.float32)
    Wres = np.asarray(inputs["L0_ffn_W_res"], np.float32)
    Wpre = np.asarray(inputs["L0_ffn_W_pre"], np.float32)
    Wpost = np.asarray(inputs["L0_ffn_W_post"], np.float32)
    Hres = np.asarray(inputs["L0_ffn_H_res_base"], np.float32)
    Hpre = np.asarray(inputs["L0_ffn_H_pre_base"], np.float32)
    Hpost = np.asarray(inputs["L0_ffn_H_post_base"], np.float32)
    Wq = np.asarray(inputs["L1_attn_Wq"], np.float32)
    Wk = np.asarray(inputs["L1_attn_Wk"], np.float32)[:, :, 0]
    Wv = np.asarray(inputs["L1_attn_Wv"], np.float32)[:, :, 0]
    A = np.asarray(inputs["L1_ffn_A"], np.float32)
    Bm = np.asarray(inputs["L1_ffn_B"], np.float32)
    Wr = np.asarray(inputs["L1_ffn_Wr"], np.float32)

    Wst = np.concatenate([Wres, Wpre, Wpost], axis=0)
    wst_h = asc(Wst.T.reshape(32, 128, 24).transpose(1, 0, 2)
                .reshape(128, 32 * 24))
    bst_h = asc(np.concatenate([Hres.reshape(16), Hpre, Hpost], axis=0))
    wbr_h = asc(np.asarray(inputs["L0_ffn_W_branch"], np.float32)
                .reshape(8, 128, 1024))
    wqh_h = asc(Wq.transpose(2, 0, 1).reshape(32, 32, 128, 128)
                .transpose(0, 2, 1, 3).reshape(32, 128, 4096))
    wkp_h = asc(Wk.reshape(32, 128, 128).transpose(1, 0, 2).reshape(128, 4096))
    wvp_h = asc(Wv.reshape(32, 128, 128).transpose(1, 0, 2).reshape(128, 4096))
    wrp_h = asc(Wr.T.reshape(32, 128, 64).transpose(1, 0, 2).reshape(128, 2048))
    at_h = asc(A.transpose(2, 0, 1).reshape(32, 128, 1024))
    bt_h = asc(Bm.transpose(0, 2, 1).reshape(1024, 4096))
    bq_h = asc(np.asarray(inputs["L1_attn_bq"], np.float32))
    bkv_h = asc(np.stack([np.asarray(inputs["L1_attn_bk"], np.float32)[:, 0],
                          np.asarray(inputs["L1_attn_bv"], np.float32)[:, 0]],
                         axis=0))
    bvq_h = asc(np.asarray(inputs["L0_attn_bv_q"], np.float32))
    bvk_h = asc(np.asarray(inputs["L0_attn_bv_k"], np.float32))
    bvv_h = asc(np.asarray(inputs["L0_ffn_gate"], np.float32)) if False else \
        asc(np.asarray(inputs["L0_attn_bv_v"], np.float32))
    gate_h = asc(np.asarray(inputs["L0_ffn_gate"], np.float32))

    maps = []
    for core in range(8):
        b, j = core // 4, core % 4
        cA, cB = 7 - j, j
        xb = x[b]
        xo_c = asc(np.concatenate([xb[512 * cA:512 * cA + 512],
                                   xb[512 * cB:512 * cB + 512]], axis=0))
        maps.append(dict(
            xt=xb, xtr=asc(xb.T), xo=xo_c, xotr=asc(xo_c.T),
            bvq=bvq_h, bvk=bvk_h, bvv=bvv_h, wst=wst_h, bst=bst_h,
            gate=gate_h, wbr=wbr_h, wqh=wqh_h, bq=bq_h, wkp=wkp_h,
            bkv=bkv_h, wvp=wvp_h, at_=at_h, bt_=bt_h, wrp=wrp_h))
    return maps


def kernel(**inputs):
    if "prog" not in _cache:
        _cache["prog"] = build_program()
    nc = _cache["prog"]
    maps = _host_maps(inputs)
    res = run_bass_kernel_spmd(nc, maps, core_ids=list(range(8)))
    y = np.empty((2, 4096, 4096), dtype=np.float32)
    for core in range(8):
        b, j = core // 4, core % 4
        cA, cB = 7 - j, j
        o = res.results[core]["out"]
        y[b, 512 * cA:512 * cA + 512] = o[0:512]
        y[b, 512 * cB:512 * cB + 512] = o[512:1024]
    return y
